# version 02
# baseline (speedup 1.0000x reference)
import numpy as np
import ml_dtypes
from contextlib import ExitStack
import concourse.bass as bass
import concourse.mybir as mybir
from concourse.bass_utils import run_bass_kernel_spmd

F32 = mybir.dt.float32
BF16 = mybir.dt.bfloat16
AF = mybir.ActivationFunctionType
ALU = mybir.AluOpType
AX = mybir.AxisListType
BF = ml_dtypes.bfloat16

NCORES = 8
B, L, D = 8, 8192, 1024
T = L
LN_EPS = 1e-5

SC_K = 64
SC_LC = L // SC_K
SC_W = 128
SC_F = B * SC_K
SC_STEPS = SC_LC + SC_W
SC_BLK = 8
SC_H = SC_F // 2


def _run(nc, in_maps):
    return run_bass_kernel_spmd(nc, in_maps, core_ids=list(range(NCORES))).results


def _nc():
    return bass.Bass("TRN2", target_bir_lowering=False, debug=False)


def _mdt(np_dtype):
    if np_dtype == np.float32:
        return F32
    if np_dtype == BF:
        return BF16
    raise ValueError(np_dtype)


def build_mm(K, Mout, act=None, relu_sq=False, extras=(), outs=(), bias=True,
             NT=512, a_np_dtype=np.float32, out_np_dtype=np.float32):
    nc = _nc()
    KC = K // 128
    MC = Mout // 128
    NTILES = T // NT
    NOUT = len(outs)
    NE = len(extras)
    ODT = _mdt(out_np_dtype)
    assert not (relu_sq and NE)
    at_d = nc.dram_tensor("at", [K, T], _mdt(a_np_dtype), kind="ExternalInput").ap()
    w_d = nc.dram_tensor("w", [K, Mout], F32, kind="ExternalInput").ap()
    bias_d = nc.dram_tensor("bias", [128, MC], F32, kind="ExternalInput").ap() if bias else None
    extra_d = [nc.dram_tensor(nm, [Mout, T], F32, kind="ExternalInput").ap()
               for (nm, _op) in extras]
    out_d = {nm: nc.dram_tensor(nm, [hi - lo, T], ODT, kind="ExternalOutput").ap()
             for (nm, lo, hi) in outs}
    if act is None:
        act = [(0, Mout, AF.Copy)]
    NBANK = 4
    n_w_dma = KC + (1 if bias else 0)

    with ExitStack() as es:
        E = es.enter_context
        w_sb = E(nc.sbuf_tensor("w_sb", [128, KC * Mout], BF16))
        b_sb = E(nc.sbuf_tensor("b_sb", [128, max(MC, 1)], F32))
        a_sb = E(nc.sbuf_tensor("a_sb", [128, 2 * KC * NT], BF16))
        e_sb = E(nc.sbuf_tensor("e_sb", [128, max(1, 2 * NE * MC * NT)], F32))
        t_sb = E(nc.sbuf_tensor("t_sb", [128, 2 * NT], F32))
        o_sb = E(nc.sbuf_tensor("o_sb", [128, 2 * MC * NT], ODT))
        ps = E(nc.psum_tensor("ps", [128, NBANK * 512], F32))
        wld = E(nc.semaphore("wld"))
        ald = [E(nc.semaphore("ald0")), E(nc.semaphore("ald1"))]
        eld = [E(nc.semaphore("eld0")), E(nc.semaphore("eld1"))]
        mmd = E(nc.semaphore("mmd"))
        sev = E(nc.semaphore("sev"))
        vev = E(nc.semaphore("vev"))
        stp = [E(nc.semaphore("st0")), E(nc.semaphore("st1"))]
        block = E(nc.Block())
        st0, st1 = stp
        @block.gpsimd
        def _(g):
            for k in range(KC):
                g.dma_start(out=w_sb[:, k * Mout:(k + 1) * Mout],
                            in_=w_d[k * 128:(k + 1) * 128, :]).then_inc(wld, 16)
            if bias_d is not None:
                g.dma_start(out=b_sb[:, 0:MC], in_=bias_d).then_inc(wld, 16)
            for n in range(NTILES):
                par = n % 2
                if n >= 2:
                    g.wait_ge(mmd, (n - 1) * MC)
                g.dma_start(
                    out=a_sb[:, par * KC * NT:(par + 1) * KC * NT]
                        .rearrange("p (k n) -> p k n", k=KC),
                    in_=at_d.rearrange("(k p) t -> p k t", p=128)[:, :, n * NT:(n + 1) * NT],
                ).then_inc(ald[par], 16)

        @block.sync
        def _(sy):
            for n in range(NTILES):
                par = n % 2
                for ei in range(NE):
                    if n >= 2 and ei == 0:
                        sy.wait_ge(vev, (n - 1) * MC)
                    sy.dma_start(
                        out=e_sb[:, (par * NE + ei) * MC * NT:(par * NE + ei + 1) * MC * NT]
                            .rearrange("p (m n) -> p m n", m=MC),
                        in_=extra_d[ei].rearrange("(m p) t -> p m t", p=128)
                            [:, :, n * NT:(n + 1) * NT],
                    ).then_inc(eld[par], 16)

        @block.tensor
        def _(t):
            t.wait_ge(wld, 16 * n_w_dma)
            for n in range(NTILES):
                t.wait_ge(ald[n % 2], 16 * (n // 2 + 1))
                for m in range(MC):
                    idx = n * MC + m
                    bank = idx % NBANK
                    if idx >= NBANK:
                        t.wait_ge(sev, idx - NBANK + 1)
                    for k in range(KC):
                        ins = t.matmul(
                            ps[:, bank * 512:bank * 512 + NT],
                            w_sb[:, k * Mout + m * 128:k * Mout + (m + 1) * 128],
                            a_sb[:, (n % 2) * KC * NT + k * NT:(n % 2) * KC * NT + (k + 1) * NT],
                            start=(k == 0), stop=(k == KC - 1))
                    ins.then_inc(mmd, 1)

        @block.scalar
        def _(s):
            dst_t = bool(NE) or relu_sq
            for n in range(NTILES):
                opar = n % 2
                for m in range(MC):
                    idx = n * MC + m
                    bank = idx % NBANK
                    s.wait_ge(mmd, idx + 1)
                    if dst_t and idx >= 2 and NE:
                        s.wait_ge(vev, idx - 1)
                    if (not NE) and n >= 2 and m == 0:
                        s.wait_ge(stp[n % 2], 16 * NOUT * (n // 2))
                    ins = None
                    for (lo, hi, fn) in act:
                        if hi <= m * 128 or lo >= (m + 1) * 128:
                            continue
                        plo = max(lo - m * 128, 0)
                        phi = min(hi - m * 128, 128)
                        tgt = (t_sb[plo:phi, (idx % 2) * NT:(idx % 2) * NT + NT] if dst_t
                               else o_sb[plo:phi, (opar * MC + m) * NT:(opar * MC + m + 1) * NT])
                        kw = {}
                        if bias_d is not None:
                            kw["bias"] = b_sb[plo:phi, m:m + 1]
                            if fn == AF.Copy:
                                fn = AF.Identity
                        ins = s.activation(tgt, ps[plo:phi, bank * 512:bank * 512 + NT],
                                           fn, **kw)
                    if relu_sq:
                        ins = s.activation(
                            o_sb[:, (opar * MC + m) * NT:(opar * MC + m + 1) * NT],
                            t_sb[:, (idx % 2) * NT:(idx % 2) * NT + NT], AF.Square)
                    ins.then_inc(sev, 1)
                if NE:
                    s.wait_ge(vev, (n + 1) * MC)
                else:
                    s.wait_ge(sev, (n + 1) * MC)
                for (nm, lo, hi) in outs:
                    if lo % 128 == 0 and hi % 128 == 0:
                        src = o_sb[:, (opar * MC + lo // 128) * NT:(opar * MC + hi // 128) * NT] \
                            .rearrange("p (m n) -> p m n", m=(hi - lo) // 128)
                        dst = out_d[nm].rearrange("(m p) t -> p m t", p=128) \
                            [:, :, n * NT:(n + 1) * NT]
                    else:
                        mc0 = lo // 128
                        src = o_sb[lo - mc0 * 128:hi - mc0 * 128,
                                   (opar * MC + mc0) * NT:(opar * MC + mc0 + 1) * NT]
                        dst = out_d[nm][:, n * NT:(n + 1) * NT]
                    s.dma_start(out=dst, in_=src).then_inc(stp[n % 2], 16)
            s.wait_ge(st0, 16 * NOUT * ((NTILES + 1) // 2))
            s.wait_ge(st1, 16 * NOUT * (NTILES // 2))

        @block.vector
        def _(v):
            if not NE:
                return
            for n in range(NTILES):
                opar = n % 2
                v.wait_ge(eld[n % 2], 16 * (n // 2 + 1) * NE)
                for m in range(MC):
                    idx = n * MC + m
                    v.wait_ge(sev, idx + 1)
                    if n >= 2 and m == 0:
                        v.wait_ge(stp[n % 2], 16 * NOUT * (n // 2))
                    src = t_sb[:, (idx % 2) * NT:(idx % 2) * NT + NT]
                    for ei, (nm, op) in enumerate(extras):
                        eap = e_sb[:, (opar * NE + ei) * MC * NT + m * NT:
                                      (opar * NE + ei) * MC * NT + (m + 1) * NT]
                        last = (ei == NE - 1)
                        dst = (o_sb[:, (opar * MC + m) * NT:(opar * MC + m + 1) * NT]
                               if last else src)
                        ins = v.tensor_tensor(dst, src, eap, op)
                    ins.then_inc(vev, 1)
    return nc


def build_ln(double):
    nc = _nc()
    NTIL = T // 128
    x_d = nc.dram_tensor("x", [T, D], F32, kind="ExternalInput").ap()
    g_d = nc.dram_tensor("gam", [128, D], F32, kind="ExternalInput").ap()
    be_d = nc.dram_tensor("bet", [128, D], F32, kind="ExternalInput").ap()
    xn_d = nc.dram_tensor("xn", [T, D], F32, kind="ExternalOutput").ap()
    z_d = nc.dram_tensor("z", [T, D], F32, kind="ExternalOutput").ap()

    with ExitStack() as es:
        E = es.enter_context
        x_sb = E(nc.sbuf_tensor("x_sb", [128, 2 * D], F32))
        xn_sb = E(nc.sbuf_tensor("xn_sb", [128, 2 * D], F32))
        z_sb = E(nc.sbuf_tensor("z_sb", [128, 2 * D], F32))
        g_sb = E(nc.sbuf_tensor("g_sb", [128, D], F32))
        be_sb = E(nc.sbuf_tensor("be_sb", [128, D], F32))
        sq_sb = E(nc.sbuf_tensor("sq_sb", [128, D], F32))
        st_sb = E(nc.sbuf_tensor("st_sb", [128, 16], F32))
        eps_sb = E(nc.sbuf_tensor("eps_sb", [128, 1], F32))
        cld = E(nc.semaphore("cld"))
        ld = [E(nc.semaphore("ld0")), E(nc.semaphore("ld1"))]
        sq1 = E(nc.semaphore("sq1"))
        sq2 = E(nc.semaphore("sq2"))
        rst = E(nc.semaphore("rst"))
        xnd = E(nc.semaphore("xnd"))
        zd = E(nc.semaphore("zd"))
        sx = [E(nc.semaphore("sx0")), E(nc.semaphore("sx1"))]
        sz = [E(nc.semaphore("sz0")), E(nc.semaphore("sz1"))]
        vv = E(nc.semaphore("vv"))
        block = E(nc.Block())
        sx0, sx1 = sx
        sz0, sz1 = sz
        @block.sync
        def _(sy):
            sy.dma_start(out=g_sb[:], in_=g_d).then_inc(cld, 16)
            sy.dma_start(out=be_sb[:], in_=be_d).then_inc(cld, 16)
            for i in range(NTIL):
                par = i % 2
                if i >= 2:
                    sy.wait_ge(xnd, i - 1)
                sy.dma_start(out=x_sb[:, par * D:(par + 1) * D],
                             in_=x_d[i * 128:(i + 1) * 128, :]).then_inc(ld[par], 16)
                sy.wait_ge(zd, i)
                if i >= 1:
                    sy.dma_start(out=z_d[(i - 1) * 128:i * 128, :],
                                 in_=z_sb[:, ((i - 1) % 2) * D:((i - 1) % 2) * D + D]
                                 ).then_inc(sz[(i - 1) % 2], 16)
            sy.wait_ge(zd, NTIL)
            sy.dma_start(out=z_d[(NTIL - 1) * 128:NTIL * 128, :],
                         in_=z_sb[:, ((NTIL - 1) % 2) * D:((NTIL - 1) % 2) * D + D]
                         ).then_inc(sz[(NTIL - 1) % 2], 16)
            sy.wait_ge(sz0, 16 * ((NTIL + 1) // 2))
            sy.wait_ge(sz1, 16 * (NTIL // 2))

        @block.gpsimd
        def _(g):
            for i in range(NTIL):
                par = i % 2
                g.wait_ge(xnd, i + 1)
                g.dma_start(out=xn_d[i * 128:(i + 1) * 128, :],
                            in_=xn_sb[:, par * D:(par + 1) * D]).then_inc(sx[par], 16)
            g.wait_ge(sx0, 16 * ((NTIL + 1) // 2))
            g.wait_ge(sx1, 16 * (NTIL // 2))

        @block.scalar
        def _(s):
            for i in range(NTIL):
                par = i % 2
                p0 = 8 * par
                s.wait_ge(cld, 32)
                s.wait_ge(ld[par], 16 * (i // 2 + 1))
                s.activation(sq_sb[:, 0:D], x_sb[:, par * D:(par + 1) * D], AF.Square,
                             accum_out=st_sb[:, p0 + 0:p0 + 1]).then_inc(sq1, 1)
                s.wait_ge(rst, (4 * i if double else 2 * i) + 1)
                s.activation(st_sb[:, p0 + 3:p0 + 4], st_sb[:, p0 + 3:p0 + 4],
                             AF.Sqrt, bias=eps_sb[:, 0:1]).then_inc(sq1, 1)
                if double:
                    s.wait_ge(xnd, i + 1)
                    s.activation(sq_sb[:, 0:D], xn_sb[:, par * D:(par + 1) * D], AF.Square,
                                 accum_out=st_sb[:, p0 + 4:p0 + 5]).then_inc(sq2, 1)
                    s.wait_ge(rst, 4 * i + 3)
                    s.activation(st_sb[:, p0 + 7:p0 + 8], st_sb[:, p0 + 7:p0 + 8],
                                 AF.Sqrt, bias=eps_sb[:, 0:1]).then_inc(sq2, 1)

        @block.vector
        def _(v):
            v.memset(eps_sb[:], LN_EPS)
            vvc = [0]

            def tick(ins):
                vvc[0] += 1
                ins.then_inc(vv, 1)

            def sync():
                v.wait_ge(vv, vvc[0])

            for i in range(NTIL):
                par = i % 2
                p0 = 8 * par
                xap = x_sb[:, par * D:(par + 1) * D]
                xnap = xn_sb[:, par * D:(par + 1) * D]
                zap = z_sb[:, par * D:(par + 1) * D]
                v.wait_ge(cld, 32)
                v.wait_ge(ld[par], 16 * (i // 2 + 1))
                tick(v.reduce_sum(st_sb[:, p0 + 1:p0 + 2], xap, axis=AX.X))
                sync()
                tick(v.tensor_scalar_mul(st_sb[:, p0 + 1:p0 + 2],
                                         st_sb[:, p0 + 1:p0 + 2], 1.0 / D))
                v.wait_ge(sq1, 2 * i + 1)
                sync()
                tick(v.tensor_mul(st_sb[:, p0 + 2:p0 + 3], st_sb[:, p0 + 1:p0 + 2],
                                  st_sb[:, p0 + 1:p0 + 2]))
                sync()
                v.scalar_tensor_tensor(st_sb[:, p0 + 3:p0 + 4], st_sb[:, p0 + 0:p0 + 1],
                                       1.0 / D, st_sb[:, p0 + 2:p0 + 3],
                                       ALU.mult, ALU.subtract).then_inc(rst, 1)
                v.wait_ge(sq1, 2 * i + 2)
                tick(v.reciprocal(st_sb[:, p0 + 3:p0 + 4], st_sb[:, p0 + 3:p0 + 4]))
                v.engine_nop().then_inc(rst, 1)
                if i >= 2:
                    v.wait_ge(sx[par], 16 * (i // 2))
                sync()
                v.tensor_scalar(xnap, xap, st_sb[:, p0 + 1:p0 + 2],
                                st_sb[:, p0 + 3:p0 + 4], ALU.subtract,
                                ALU.mult).then_inc(xnd, 1)
                if i >= 2:
                    v.wait_ge(sz[par], 16 * (i // 2))
                if double:
                    tick(v.reduce_sum(st_sb[:, p0 + 5:p0 + 6], xnap, axis=AX.X))
                    sync()
                    tick(v.tensor_scalar_mul(st_sb[:, p0 + 5:p0 + 6],
                                             st_sb[:, p0 + 5:p0 + 6], 1.0 / D))
                    v.wait_ge(sq2, 2 * i + 1)
                    sync()
                    tick(v.tensor_mul(st_sb[:, p0 + 6:p0 + 7], st_sb[:, p0 + 5:p0 + 6],
                                      st_sb[:, p0 + 5:p0 + 6]))
                    sync()
                    v.scalar_tensor_tensor(st_sb[:, p0 + 7:p0 + 8], st_sb[:, p0 + 4:p0 + 5],
                                           1.0 / D, st_sb[:, p0 + 6:p0 + 7],
                                           ALU.mult, ALU.subtract).then_inc(rst, 1)
                    v.wait_ge(sq2, 2 * i + 2)
                    tick(v.reciprocal(st_sb[:, p0 + 7:p0 + 8], st_sb[:, p0 + 7:p0 + 8]))
                    v.engine_nop().then_inc(rst, 1)
                    sync()
                    v.tensor_scalar(zap, xnap, st_sb[:, p0 + 5:p0 + 6],
                                    st_sb[:, p0 + 7:p0 + 8], ALU.subtract, ALU.mult)
                    v.tensor_mul(zap, zap, g_sb[:])
                else:
                    v.tensor_mul(zap, xnap, g_sb[:])
                v.tensor_add(zap, zap, be_sb[:]).then_inc(zd, 1)
    return nc


def build_scan0():
    nc = _nc()
    NBLK = SC_STEPS // SC_BLK
    BLKC = SC_BLK * SC_F
    WBLK = SC_W // SC_BLK
    H = SC_H
    u_d = nc.dram_tensor("u", [128, SC_STEPS * SC_F], BF16, kind="ExternalInput").ap()
    g_d = nc.dram_tensor("g", [128, SC_STEPS * SC_F], BF16, kind="ExternalInput").ap()
    a_d = nc.dram_tensor("a", [128, 1], F32, kind="ExternalInput").ap()
    hs_d = nc.dram_tensor("hs", [128, SC_LC * SC_F], BF16, kind="ExternalOutput").ap()

    with ExitStack() as es:
        E = es.enter_context
        u_sb = E(nc.sbuf_tensor("u_sb", [128, 2 * BLKC], F32))
        g_sb = E(nc.sbuf_tensor("g_sb", [128, 2 * BLKC], F32))
        h_sb = E(nc.sbuf_tensor("h_sb", [128, 2 * BLKC], F32))
        a_sb = E(nc.sbuf_tensor("a_sb", [128, 1], F32))
        z_sb = E(nc.sbuf_tensor("z_sb", [128, SC_F], F32))
        v_sb = E(nc.sbuf_tensor("v_sb", [128, SC_F], F32))
        c_sb = E(nc.sbuf_tensor("c_sb", [128, SC_F], F32))
        d_sb = E(nc.sbuf_tensor("d_sb", [128, SC_F], F32))
        cld = E(nc.semaphore("cld"))
        ld = [E(nc.semaphore("ld0")), E(nc.semaphore("ld1"))]
        vs = E(nc.semaphore("vs"))
        cs = E(nc.semaphore("cs"))
        stp = [E(nc.semaphore("st0")), E(nc.semaphore("st1"))]
        block = E(nc.Block())
        st0, st1 = stp

        def st_after(sblk):
            return 16 * (sblk // 2 + 1)

        @block.sync
        def _(sy):
            sy.dma_start(out=a_sb[:], in_=a_d).then_inc(cld, 16)
            sy.wait_ge(st0, st_after(NBLK - WBLK - 2))
            sy.wait_ge(st1, st_after(NBLK - WBLK - 1))

        @block.gpsimd
        def _(g):
            for blk in range(NBLK):
                par = blk % 2
                if blk >= 2:
                    g.wait_ge(vs, 4 * SC_BLK * (blk - 1))
                    if blk - 2 >= WBLK:
                        g.dma_start(
                            out=hs_d[:, (blk - 2 - WBLK) * BLKC:(blk - 1 - WBLK) * BLKC],
                            in_=h_sb[:, par * BLKC:par * BLKC + BLKC]
                        ).then_inc(stp[(blk - 2 - WBLK) % 2], 16)
                g.dma_start(out=u_sb[:, par * BLKC:(par + 1) * BLKC],
                            in_=u_d[:, blk * BLKC:(blk + 1) * BLKC]).then_inc(ld[par], 16)
                g.dma_start(out=g_sb[:, par * BLKC:(par + 1) * BLKC],
                            in_=g_d[:, blk * BLKC:(blk + 1) * BLKC]).then_inc(ld[par], 16)
            for blk in range(NBLK - 2, NBLK):
                g.wait_ge(vs, 4 * SC_BLK * (blk + 1))
                g.dma_start(out=hs_d[:, (blk - WBLK) * BLKC:(blk - WBLK + 1) * BLKC],
                            in_=h_sb[:, (blk % 2) * BLKC:(blk % 2) * BLKC + BLKC]
                            ).then_inc(stp[(blk - WBLK) % 2], 16)
            g.wait_ge(st0, st_after(NBLK - WBLK - 2))
            g.wait_ge(st1, st_after(NBLK - WBLK - 1))

        @block.vector
        def _(v):
            v.memset(z_sb[:], 0.0)
            for s in range(SC_STEPS):
                blk = s // SC_BLK
                sl = s % SC_BLK
                par = blk % 2
                if sl == 0:
                    if blk == 0:
                        v.wait_ge(cld, 16)
                    v.wait_ge(ld[par], 32 * (blk // 2 + 1))
                    if blk >= 2 and blk - 2 >= WBLK:
                        v.wait_ge(stp[(blk - 2 - WBLK) % 2], st_after(blk - 2 - WBLK))
                if s == 0:
                    hprev = z_sb[:]
                elif sl == 0:
                    hprev = h_sb[:, (1 - par) * BLKC + (SC_BLK - 1) * SC_F:
                                    (1 - par) * BLKC + SC_BLK * SC_F]
                else:
                    hprev = h_sb[:, par * BLKC + (sl - 1) * SC_F:par * BLKC + sl * SC_F]
                uap = u_sb[:, par * BLKC + sl * SC_F:par * BLKC + (sl + 1) * SC_F]
                gap = g_sb[:, par * BLKC + sl * SC_F:par * BLKC + (sl + 1) * SC_F]
                hap = h_sb[:, par * BLKC + sl * SC_F:par * BLKC + (sl + 1) * SC_F]
                for hf in range(2):
                    c0, c1 = hf * H, (hf + 1) * H
                    if s >= 1:
                        v.wait_ge(cs, 2 * (s - 1) + hf + 1)
                    v.scalar_tensor_tensor(v_sb[:, c0:c1], hprev[:, c0:c1], a_sb[:, 0:1],
                                           uap[:, c0:c1], ALU.mult, ALU.add).then_inc(vs, 1)
                for hf in range(2):
                    c0, c1 = hf * H, (hf + 1) * H
                    v.wait_ge(cs, 2 * s + hf + 1)
                    v.tensor_sub(d_sb[:, c0:c1], hprev[:, c0:c1], c_sb[:, c0:c1])
                    v.tensor_mul(d_sb[:, c0:c1], gap[:, c0:c1], d_sb[:, c0:c1])
                    v.tensor_add(hap[:, c0:c1], d_sb[:, c0:c1], c_sb[:, c0:c1]).then_inc(vs, 1)

        @block.scalar
        def _(s_):
            for s in range(SC_STEPS):
                for hf in range(2):
                    c0, c1 = hf * H, (hf + 1) * H
                    s_.wait_ge(vs, 4 * s + 1 + hf)
                    s_.activation(c_sb[:, c0:c1], v_sb[:, c0:c1],
                                  AF.Tanh).then_inc(cs, 1)
    return nc


def build_scan1():
    nc = _nc()
    a_d = nc.dram_tensor("a", [128, L], F32, kind="ExternalInput").ap()
    u_d = nc.dram_tensor("u", [128, L], F32, kind="ExternalInput").ap()
    s_d = nc.dram_tensor("states", [128, L], F32, kind="ExternalOutput").ap()
    with (
        nc.sbuf_tensor("a_sb", [128, L], F32) as a_sb,
        nc.sbuf_tensor("u_sb", [128, L], F32) as u_sb,
        nc.sbuf_tensor("o_sb", [128, L], F32) as o_sb,
        nc.semaphore("ld") as ld,
        nc.semaphore("dn") as dn,
        nc.semaphore("st") as st,
        nc.Block() as block,
    ):
        @block.sync
        def _(sy):
            sy.dma_start(out=a_sb[:], in_=a_d).then_inc(ld, 16)
            sy.dma_start(out=u_sb[:], in_=u_d).then_inc(ld, 16)
            sy.wait_ge(dn, 1)
            sy.dma_start(out=s_d, in_=o_sb[:]).then_inc(st, 16)
            sy.wait_ge(st, 16)

        @block.vector
        def _(v):
            v.wait_ge(ld, 32)
            v.tensor_tensor_scan(o_sb[:], a_sb[:], u_sb[:], 0.0,
                                 ALU.mult, ALU.add).then_inc(dn, 1)
    return nc


def build_y_out(NT=256):
    nc = _nc()
    NTILES = T // NT
    MC = D // 128
    st_d = nc.dram_tensor("states", [16, T], F32, kind="ExternalInput").ap()
    c_d = nc.dram_tensor("c_ssm", [2048, 16], F32, kind="ExternalInput").ap()
    wo_d = nc.dram_tensor("woT", [2048, D], F32, kind="ExternalInput").ap()
    bo_d = nc.dram_tensor("b_out", [128, 8], F32, kind="ExternalInput").ap()
    x1_d = nc.dram_tensor("x1", [D, T], F32, kind="ExternalInput").ap()
    xn1_d = nc.dram_tensor("xn1", [D, T], F32, kind="ExternalInput").ap()
    s1_d = nc.dram_tensor("s1", [D, T], F32, kind="ExternalOutput").ap()

    with ExitStack() as es:
        E = es.enter_context
        c_sb = E(nc.sbuf_tensor("c_sb", [128, 16 * 16], BF16))
        wo_sb = E(nc.sbuf_tensor("wo_sb", [128, 16 * D], BF16))
        gf_sb = E(nc.sbuf_tensor("gf_sb", [16, D], F32))
        g_sb = E(nc.sbuf_tensor("g_sb", [16, D], BF16))
        sts = E(nc.sbuf_tensor("sts", [16, L], BF16))
        bo_sb = E(nc.sbuf_tensor("bo_sb", [128, MC], F32))
        x1_sb = E(nc.sbuf_tensor("x1_sb", [128, 2 * MC * NT], F32))
        xn1_sb = E(nc.sbuf_tensor("xn1_sb", [128, 2 * MC * NT], F32))
        t_sb = E(nc.sbuf_tensor("t_sb", [128, 2 * NT], F32))
        o_sb = E(nc.sbuf_tensor("o_sb", [128, 2 * MC * NT], F32))
        psg = E(nc.psum_tensor("psg", [16, 1024], F32))
        ps = E(nc.psum_tensor("ps", [128, 4 * 512], F32))
        wld = E(nc.semaphore("wld"))
        eld = [E(nc.semaphore("eld0")), E(nc.semaphore("eld1"))]
        gdn = E(nc.semaphore("gdn"))
        mmd = E(nc.semaphore("mmd"))
        sev = E(nc.semaphore("sev"))
        vev = E(nc.semaphore("vev"))
        stp = [E(nc.semaphore("st0")), E(nc.semaphore("st1"))]
        block = E(nc.Block())
        st0, st1 = stp
        @block.gpsimd
        def _(g):
            for k in range(16):
                g.dma_start(out=c_sb[:, k * 16:(k + 1) * 16],
                            in_=c_d[k * 128:(k + 1) * 128, :]).then_inc(wld, 16)
            for k in range(16):
                g.dma_start(out=wo_sb[:, k * D:(k + 1) * D],
                            in_=wo_d[k * 128:(k + 1) * 128, :]).then_inc(wld, 16)
            g.dma_start(out=bo_sb[:, 0:MC], in_=bo_d).then_inc(wld, 16)
            g.dma_start(out=sts[:], in_=st_d).then_inc(wld, 16)

        @block.sync
        def _(sy):
            for n in range(NTILES):
                par = n % 2
                if n >= 2:
                    sy.wait_ge(vev, (n - 1) * MC)
                sy.dma_start(out=x1_sb[:, par * MC * NT:(par + 1) * MC * NT]
                             .rearrange("p (m n) -> p m n", m=MC),
                             in_=x1_d.rearrange("(m p) t -> p m t", p=128)
                             [:, :, n * NT:(n + 1) * NT]).then_inc(eld[par], 16)
                sy.dma_start(out=xn1_sb[:, par * MC * NT:(par + 1) * MC * NT]
                             .rearrange("p (m n) -> p m n", m=MC),
                             in_=xn1_d.rearrange("(m p) t -> p m t", p=128)
                             [:, :, n * NT:(n + 1) * NT]).then_inc(eld[par], 16)

        @block.tensor
        def _(t):
            t.wait_ge(wld, 16 * 34)
            for k in range(16):
                ins = t.matmul(psg[:, 0:512], c_sb[:, k * 16:(k + 1) * 16],
                               wo_sb[:, k * D:k * D + 512], start=(k == 0), stop=(k == 15))
            for k in range(16):
                ins = t.matmul(psg[:, 512:1024], c_sb[:, k * 16:(k + 1) * 16],
                               wo_sb[:, k * D + 512:(k + 1) * D],
                               start=(k == 0), stop=(k == 15))
            ins.then_inc(mmd, 1)
            t.wait_ge(gdn, 1)
            for n in range(NTILES):
                for m in range(MC):
                    idx = n * MC + m
                    bank = idx % 4
                    if idx >= 4:
                        t.wait_ge(sev, idx - 3)
                    t.matmul(ps[:, bank * 512:bank * 512 + NT],
                             g_sb[:, m * 128:(m + 1) * 128],
                             sts[:, n * NT:(n + 1) * NT],
                             start=True, stop=True).then_inc(mmd, 1)

        @block.scalar
        def _(s):
            for n in range(NTILES):
                for m in range(MC):
                    idx = n * MC + m
                    bank = idx % 4
                    s.wait_ge(mmd, idx + 2)
                    if idx >= 2:
                        s.wait_ge(vev, idx - 1)
                    s.activation(t_sb[:, (idx % 2) * NT:(idx % 2) * NT + NT],
                                 ps[:, bank * 512:bank * 512 + NT], AF.Identity,
                                 bias=bo_sb[:, m:m + 1]).then_inc(sev, 1)
                s.wait_ge(vev, (n + 1) * MC)
                s.dma_start(out=s1_d.rearrange("(m p) t -> p m t", p=128)
                            [:, :, n * NT:(n + 1) * NT],
                            in_=o_sb[:, (n % 2) * MC * NT:((n % 2) + 1) * MC * NT]
                            .rearrange("p (m n) -> p m n", m=MC)).then_inc(stp[n % 2], 16)
            s.wait_ge(st0, 16 * ((NTILES + 1) // 2))
            s.wait_ge(st1, 16 * (NTILES // 2))

        @block.vector
        def _(v):
            v.wait_ge(mmd, 1)
            v.tensor_copy(gf_sb[:], psg[:, 0:1024])
            v.tensor_copy(g_sb[:], gf_sb[:]).then_inc(gdn, 1)
            for n in range(NTILES):
                opar = n % 2
                v.wait_ge(eld[n % 2], 16 * 2 * (n // 2 + 1))
                for m in range(MC):
                    idx = n * MC + m
                    v.wait_ge(sev, idx + 1)
                    if n >= 2 and m == 0:
                        v.wait_ge(stp[n % 2], 16 * (n // 2))
                    src = t_sb[:, (idx % 2) * NT:(idx % 2) * NT + NT]
                    v.tensor_tensor(src, src,
                                    xn1_sb[:, (opar * MC + m) * NT:(opar * MC + m + 1) * NT],
                                    ALU.add)
                    v.tensor_tensor(o_sb[:, (opar * MC + m) * NT:(opar * MC + m + 1) * NT],
                                    src,
                                    x1_sb[:, (opar * MC + m) * NT:(opar * MC + m + 1) * NT],
                                    ALU.add).then_inc(vev, 1)
    return nc


def build_shift():
    nc = _nc()
    MC = D // 128
    x_d = nc.dram_tensor("x", [D, T], F32, kind="ExternalInput").ap()
    mur_d = nc.dram_tensor("mur", [128, 8], F32, kind="ExternalInput").ap()
    muk_d = nc.dram_tensor("muk", [128, 8], F32, kind="ExternalInput").ap()
    xr_d = nc.dram_tensor("xr", [D, T], F32, kind="ExternalOutput").ap()
    xk_d = nc.dram_tensor("xk", [D, T], F32, kind="ExternalOutput").ap()
    with (
        nc.sbuf_tensor("x_sb", [128, T], F32) as x_sb,
        nc.sbuf_tensor("d_sb", [128, T], F32) as d_sb,
        nc.sbuf_tensor("o_sb", [128, T], F32) as o_sb,
        nc.sbuf_tensor("mu_sb", [128, 2 * MC], F32) as mu_sb,
        nc.semaphore("ld") as ld,
        nc.semaphore("dn") as dn,
        nc.semaphore("st") as st,
        nc.Block() as block,
    ):
        @block.sync
        def _(sy):
            sy.dma_start(out=mu_sb[:, 0:MC], in_=mur_d).then_inc(ld, 16)
            sy.dma_start(out=mu_sb[:, MC:2 * MC], in_=muk_d).then_inc(ld, 16)
            for m in range(MC):
                if m >= 1:
                    sy.wait_ge(dn, 3 * m)
                sy.dma_start(out=x_sb[:], in_=x_d[m * 128:(m + 1) * 128, :]).then_inc(ld, 16)
            sy.wait_ge(st, 16 * 2 * MC)

        @block.vector
        def _(v):
            for m in range(MC):
                v.wait_ge(ld, 16 * (m + 3))
                if m >= 1:
                    v.wait_ge(st, 16 * 2 * m)
                v.tensor_sub(d_sb[:, 1:T], x_sb[:, 1:T], x_sb[:, 0:T - 1])
                v.tensor_copy(d_sb[:, 0:1], x_sb[:, 0:1]).then_inc(dn, 1)
                v.scalar_tensor_tensor(o_sb[:, 1:T], d_sb[:, 1:T], mu_sb[:, m:m + 1],
                                       x_sb[:, 0:T - 1], ALU.mult, ALU.add)
                v.tensor_scalar_mul(o_sb[:, 0:1], d_sb[:, 0:1],
                                    mu_sb[:, m:m + 1]).then_inc(dn, 1)
                v.wait_ge(st, 16 * (2 * m + 1))
                v.scalar_tensor_tensor(o_sb[:, 1:T], d_sb[:, 1:T], mu_sb[:, MC + m:MC + m + 1],
                                       x_sb[:, 0:T - 1], ALU.mult, ALU.add)
                v.tensor_scalar_mul(o_sb[:, 0:1], d_sb[:, 0:1],
                                    mu_sb[:, MC + m:MC + m + 1]).then_inc(dn, 1)

        @block.scalar
        def _(s):
            for m in range(MC):
                s.wait_ge(dn, 3 * m + 2)
                s.dma_start(out=xr_d[m * 128:(m + 1) * 128, :], in_=o_sb[:]).then_inc(st, 16)
                s.wait_ge(dn, 3 * m + 3)
                s.dma_start(out=xk_d[m * 128:(m + 1) * 128, :], in_=o_sb[:]).then_inc(st, 16)
            s.wait_ge(st, 16 * 2 * MC)
    return nc


def build_ffn1_loop(RC=512, NB=B):
    nc = _nc()
    NPASS = 1024 // RC
    MC = D // 128
    NT = RC
    assert NT <= 512
    xw_d = nc.dram_tensor("xw", [3 * D, B * 1024], BF16, kind="ExternalInput").ap()
    u_d = nc.dram_tensor("u3", [D, 3 * D], F32, kind="ExternalInput").ap()
    b_d = nc.dram_tensor("b3", [128, 24], F32, kind="ExternalInput").ap()
    c0_d = nc.dram_tensor("c0", [D, 1024], F32, kind="ExternalInput").ap()
    id_d = nc.dram_tensor("ident", [128, 128], F32, kind="ExternalInput").ap()
    cf_d = nc.dram_tensor("cf", [D, 1024], F32, kind="ExternalOutput").ap()

    def mm_after(step, m, gi):
        return step * 3 * MC + m * 3 + gi + 1

    def upd_after(p, t, m):
        return p * (1 + NB * MC) + 1 + t * MC + m + 1

    def upd_init(p):
        return p * (1 + NB * MC) + 1

    with (
        nc.sbuf_tensor("u_sb", [128, MC * 3 * D], BF16) as u_sb,
        nc.sbuf_tensor("id_sb", [128, 128], BF16) as id_sb,
        nc.sbuf_tensor("b_sb", [128, 3 * MC], F32) as b_sb,
        nc.sbuf_tensor("c_sb", [128, MC * RC], F32) as c_sb,
        nc.sbuf_tensor("cb_sb", [128, 2 * MC * RC], BF16) as cb_sb,
        nc.sbuf_tensor("xw_sb", [128, 2 * 3 * MC * RC], BF16) as xw_sb,
        nc.sbuf_tensor("ga_sb", [128, 2 * 3 * NT], F32) as ga_sb,
        nc.psum_tensor("ps", [128, 8 * 512], F32) as ps,
        nc.semaphore("wld") as wld,
        nc.semaphore("xld0") as xld0,
        nc.semaphore("xld1") as xld1,
        nc.semaphore("mmd") as mmd,
        nc.semaphore("gev") as gev,
        nc.semaphore("upd") as upd,
        nc.semaphore("st") as st,
        nc.Block() as block,
    ):
        xld = [xld0, xld1]
        @block.gpsimd
        def _(g):
            for k in range(MC):
                g.dma_start(out=u_sb[:, k * 3 * D:(k + 1) * 3 * D],
                            in_=u_d[k * 128:(k + 1) * 128, :]).then_inc(wld, 16)
            g.dma_start(out=id_sb[:], in_=id_d).then_inc(wld, 16)
            g.dma_start(out=b_sb[:, 0:3 * MC], in_=b_d).then_inc(wld, 16)
            for p in range(NPASS):
                if p >= 1:
                    g.wait_ge(st, 16 * p)
                g.dma_start(out=c_sb[:].rearrange("p (m r) -> p m r", m=MC),
                            in_=c0_d.rearrange("(m p) r -> p m r", p=128)
                            [:, :, p * RC:(p + 1) * RC]).then_inc(wld, 16)
                for t in range(NB):
                    step = p * NB + t
                    if step >= 2:
                        g.wait_ge(gev, (step - 1) * 3 * MC)
                    for gi in range(3):
                        g.dma_start(
                            out=xw_sb[:, ((t % 2) * 3 + gi) * MC * RC:
                                         ((t % 2) * 3 + gi + 1) * MC * RC]
                                .rearrange("p (m r) -> p m r", m=MC),
                            in_=xw_d.rearrange("(m p) t -> p m t", p=128)
                                [:, gi * MC:(gi + 1) * MC,
                                 t * 1024 + p * RC:t * 1024 + (p + 1) * RC],
                        ).then_inc(xld[t % 2], 16)
                g.wait_ge(upd, upd_after(p, NB - 1, MC - 1))
                g.dma_start(out=cf_d.rearrange("(m p) r -> p m r", p=128)
                            [:, :, p * RC:(p + 1) * RC],
                            in_=c_sb[:].rearrange("p (m r) -> p m r", m=MC)).then_inc(st, 16)
            g.wait_ge(st, 16 * NPASS)

        @block.tensor
        def _(t_):
            t_.wait_ge(wld, 16 * (MC + 2))
            for p in range(NPASS):
                t_.wait_ge(wld, 16 * (MC + 2 + p + 1))
                for t in range(NB):
                    step = p * NB + t
                    t_.wait_ge(xld[t % 2], 48 * (step // 2 + 1))
                    t_.wait_ge(upd, upd_after(p, t - 1, MC - 1) if t > 0 else upd_init(p))
                    for m in range(MC):
                        for gi in range(3):
                            bidx = gi * 2 + (m % 2)
                            if m >= 2:
                                t_.wait_ge(gev, mm_after(step, m - 2, gi))
                            elif step >= 1:
                                t_.wait_ge(gev, mm_after(step - 1, MC - 2 + m, gi))
                            t_.matmul(ps[:, bidx * 512:bidx * 512 + NT],
                                      id_sb[:],
                                      xw_sb[:, ((t % 2) * 3 + gi) * MC * RC + m * RC:
                                               ((t % 2) * 3 + gi) * MC * RC + m * RC + NT],
                                      start=True, stop=False)
                            for k in range(MC):
                                ins = t_.matmul(
                                    ps[:, bidx * 512:bidx * 512 + NT],
                                    u_sb[:, k * 3 * D + gi * D + m * 128:
                                            k * 3 * D + gi * D + (m + 1) * 128],
                                    cb_sb[:, ((step - 1) % 2) * MC * RC + k * RC:
                                             ((step - 1) % 2) * MC * RC + k * RC + NT],
                                    start=False, stop=(k == MC - 1))
                            ins.then_inc(mmd, 1)

        @block.scalar
        def _(s):
            AFS = [AF.Sigmoid, AF.Sigmoid, AF.Tanh]
            for p in range(NPASS):
                for t in range(NB):
                    step = p * NB + t
                    for m in range(MC):
                        for gi in range(3):
                            s.wait_ge(mmd, mm_after(step, m, gi))
                            if m >= 1:
                                s.wait_ge(upd, upd_after(p, t, m - 1))
                            elif step >= 1:
                                pp, tt = (p, t - 1) if t > 0 else (p - 1, NB - 1)
                                s.wait_ge(upd, upd_after(pp, tt, MC - 1))
                            bidx = gi * 2 + (m % 2)
                            s.activation(
                                ga_sb[:, ((step % 2) * 3 + gi) * NT:((step % 2) * 3 + gi) * NT + NT],
                                ps[:, bidx * 512:bidx * 512 + NT],
                                AFS[gi], bias=b_sb[:, gi * MC + m:gi * MC + m + 1]
                            ).then_inc(gev, 1)

        @block.vector
        def _(v):
            for p in range(NPASS):
                v.wait_ge(wld, 16 * (MC + 2 + p + 1))
                for k in range(MC):
                    ins = v.tensor_copy(cb_sb[:, MC * RC + k * RC:MC * RC + (k + 1) * RC],
                                        c_sb[:, k * RC:(k + 1) * RC])
                ins.then_inc(upd, 1)
                for t in range(NB):
                    step = p * NB + t
                    for m in range(MC):
                        v.wait_ge(gev, step * 3 * MC + m * 3 + 3)
                        sp = (step % 2) * 3
                        f_ap = ga_sb[:, (sp + 0) * NT:(sp + 0) * NT + NT]
                        i_ap = ga_sb[:, (sp + 1) * NT:(sp + 1) * NT + NT]
                        g_ap = ga_sb[:, (sp + 2) * NT:(sp + 2) * NT + NT]
                        c_ap = c_sb[:, m * RC:m * RC + NT]
                        cb_ap = cb_sb[:, (step % 2) * MC * RC + m * RC:
                                         (step % 2) * MC * RC + m * RC + NT]
                        v.tensor_mul(c_ap, f_ap, c_ap)
                        v.tensor_mul(i_ap, i_ap, g_ap)
                        v.tensor_add(c_ap, c_ap, i_ap)
                        v.tensor_copy(cb_ap, c_ap).then_inc(upd, 1)
    return nc


def build_ffn1_out():
    nc = _nc()
    MC = D // 128
    RT = 1024
    NT = 512
    NTL = RT // NT
    cf_d = nc.dram_tensor("cf", [D, RT], F32, kind="ExternalInput").ap()
    uo_d = nc.dram_tensor("uoT", [D, D], F32, kind="ExternalInput").ap()
    xwo_d = nc.dram_tensor("xwo", [D, B * RT], BF16, kind="ExternalInput").ap()
    bo_d = nc.dram_tensor("bo", [128, 8], F32, kind="ExternalInput").ap()
    s1_d = nc.dram_tensor("s1", [D, B * RT], F32, kind="ExternalInput").ap()
    x2_d = nc.dram_tensor("x2", [D, B * RT], F32, kind="ExternalOutput").ap()

    with ExitStack() as es:
        E = es.enter_context
        uo_sb = E(nc.sbuf_tensor("uo_sb", [128, MC * D], BF16))
        cb_sb = E(nc.sbuf_tensor("cb_sb", [128, MC * RT], BF16))
        bo_sb = E(nc.sbuf_tensor("bo_sb", [128, MC], F32))
        cuo_sb = E(nc.sbuf_tensor("cuo_sb", [128, MC * RT], F32))
        xwo_sb = E(nc.sbuf_tensor("xwo_sb", [128, MC * RT], F32))
        s1_sb = E(nc.sbuf_tensor("s1_sb", [128, MC * RT], F32))
        t_sb = E(nc.sbuf_tensor("t_sb", [128, 2 * NT], F32))
        t2_sb = E(nc.sbuf_tensor("t2_sb", [128, 2 * NT], F32))
        o_sb = E(nc.sbuf_tensor("o_sb", [128, MC * RT], F32))
        ps = E(nc.psum_tensor("ps", [128, 4 * 512], F32))
        wld = E(nc.semaphore("wld"))
        xls = E(nc.semaphore("xls"))
        sls = E(nc.semaphore("sls"))
        mmd = E(nc.semaphore("mmd"))
        cud = E(nc.semaphore("cud"))
        vad = E(nc.semaphore("vad"))
        tad = E(nc.semaphore("tad"))
        oad = E(nc.semaphore("oad"))
        st = E(nc.semaphore("st"))
        block = E(nc.Block())
        @block.gpsimd
        def _(g):
            for k in range(MC):
                g.dma_start(out=uo_sb[:, k * D:(k + 1) * D],
                            in_=uo_d[k * 128:(k + 1) * 128, :]).then_inc(wld, 16)
            g.dma_start(out=cb_sb[:].rearrange("p (m r) -> p m r", m=MC),
                        in_=cf_d.rearrange("(m p) r -> p m r", p=128)).then_inc(wld, 16)
            g.dma_start(out=bo_sb[:, 0:MC], in_=bo_d).then_inc(wld, 16)
            for t in range(B):
                if t >= 1:
                    g.wait_ge(vad, t * MC * NTL)
                g.dma_start(out=xwo_sb[:].rearrange("p (m r) -> p m r", m=MC),
                            in_=xwo_d.rearrange("(m p) t -> p m t", p=128)
                            [:, :, t * RT:(t + 1) * RT]).then_inc(xls, 16)
            g.wait_ge(st, 16 * B)

        @block.sync
        def _(sy):
            for t in range(B):
                if t >= 1:
                    sy.wait_ge(oad, t * MC * NTL)
                sy.dma_start(out=s1_sb[:].rearrange("p (m r) -> p m r", m=MC),
                             in_=s1_d.rearrange("(m p) t -> p m t", p=128)
                             [:, :, t * RT:(t + 1) * RT]).then_inc(sls, 16)

        @block.tensor
        def _(t_):
            t_.wait_ge(wld, 16 * (MC + 2))
            for m in range(MC):
                for ntl in range(NTL):
                    idx = m * NTL + ntl
                    bank = idx % 4
                    if idx >= 4:
                        t_.wait_ge(cud, idx - 3)
                    for k in range(MC):
                        ins = t_.matmul(ps[:, bank * 512:bank * 512 + NT],
                                        uo_sb[:, k * D + m * 128:k * D + (m + 1) * 128],
                                        cb_sb[:, k * RT + ntl * NT:k * RT + (ntl + 1) * NT],
                                        start=(k == 0), stop=(k == MC - 1))
                    ins.then_inc(mmd, 1)

        @block.vector
        def _(v):
            for m in range(MC):
                for ntl in range(NTL):
                    idx = m * NTL + ntl
                    bank = idx % 4
                    v.wait_ge(mmd, idx + 1)
                    v.tensor_copy(cuo_sb[:, m * RT + ntl * NT:m * RT + (ntl + 1) * NT],
                                  ps[:, bank * 512:bank * 512 + NT]).then_inc(cud, 1)
            for t in range(B):
                v.wait_ge(xls, 16 * (t + 1))
                v.wait_ge(sls, 16 * (t + 1))
                if t >= 1:
                    v.wait_ge(st, 16 * t)
                for m in range(MC):
                    for ntl in range(NTL):
                        idx = t * MC * NTL + m * NTL + ntl
                        tpar = idx % 2
                        if idx >= 2:
                            v.wait_ge(tad, idx - 1)
                        v.tensor_add(t_sb[:, tpar * NT:(tpar + 1) * NT],
                                     xwo_sb[:, m * RT + ntl * NT:m * RT + (ntl + 1) * NT],
                                     cuo_sb[:, m * RT + ntl * NT:m * RT + (ntl + 1) * NT]
                                     ).then_inc(vad, 1)
                        v.wait_ge(tad, idx + 1)
                        v.tensor_add(o_sb[:, m * RT + ntl * NT:m * RT + (ntl + 1) * NT],
                                     t2_sb[:, tpar * NT:(tpar + 1) * NT],
                                     s1_sb[:, m * RT + ntl * NT:m * RT + (ntl + 1) * NT]
                                     ).then_inc(oad, 1)

        @block.scalar
        def _(s):
            for t in range(B):
                for m in range(MC):
                    for ntl in range(NTL):
                        idx = t * MC * NTL + m * NTL + ntl
                        tpar = idx % 2
                        s.wait_ge(vad, idx + 1)
                        if idx >= 2:
                            s.wait_ge(oad, idx - 1)
                        s.activation(t2_sb[:, tpar * NT:(tpar + 1) * NT],
                                     t_sb[:, tpar * NT:(tpar + 1) * NT],
                                     AF.Tanh, bias=bo_sb[:, m:m + 1]).then_inc(tad, 1)
                s.wait_ge(oad, (t + 1) * MC * NTL)
                s.dma_start(out=x2_d.rearrange("(m p) t -> p m t", p=128)
                            [:, :, t * RT:(t + 1) * RT],
                            in_=o_sb[:].rearrange("p (m r) -> p m r", m=MC)
                            ).then_inc(st, 16)
            s.wait_ge(st, 16 * B)
    return nc


def _bc(v):
    return np.ascontiguousarray(np.broadcast_to(np.asarray(v, np.float32), (128, D)))


def _col(v):
    a = np.asarray(v, np.float32)
    return np.ascontiguousarray(a.reshape(-1, 128).T)


def _t(a):
    return np.ascontiguousarray(a.T)


_DBG = None


def _dbg(name, val):
    if _DBG is not None:
        _DBG(name, val)


def kernel(**inp):
    x = np.asarray(inp["L0_attn_x"], np.float32)

    nc = build_ln(True)
    maps = [dict(x=np.ascontiguousarray(x[b]), gam=_bc(inp["L0_attn_gamma"]),
                 bet=_bc(inp["L0_attn_beta"])) for b in range(B)]
    r = _run(nc, maps)
    xn0 = [r[b]["xn"] for b in range(B)]
    z = [r[b]["z"] for b in range(B)]
    _dbg("xn0", xn0); _dbg("z", z)

    nc = build_mm(D, 2 * D, act=[(0, D, AF.Copy), (D, 2 * D, AF.Sigmoid)],
                  outs=[("u", 0, D), ("g", D, 2 * D)], bias=True,
                  out_np_dtype=BF)
    maps = [dict(at=_t(z[b]), w=np.asarray(inp["L0_attn_W_in"], np.float32),
                 bias=_col(inp["L0_attn_b_in"])) for b in range(B)]
    r = _run(nc, maps)
    uT = [r[b]["u"] for b in range(B)]
    gT = [r[b]["g"] for b in range(B)]
    _dbg("uT", uT); _dbg("gT", gT)

    A = np.asarray(inp["L0_attn_A"], np.float32)
    nc = build_scan0()
    maps = []
    for c in range(NCORES):
        U = _scan0_pack(uT, c)
        G = _scan0_pack(gT, c)
        maps.append(dict(u=U, g=G, a=_col(A[c * 128:(c + 1) * 128])))
    r = _run(nc, maps)
    hsT = _scan0_unpack([r[c]["hs"] for c in range(NCORES)])
    _dbg("hsT", hsT)

    nc = build_mm(D, D, act=None, outs=[("o", 0, D)], bias=True, a_np_dtype=BF)
    maps = [dict(at=hsT[b], w=np.asarray(inp["L0_attn_W_out"], np.float32),
                 bias=_col(inp["L0_attn_b_out"])) for b in range(B)]
    r = _run(nc, maps)
    attn0T = [r[b]["o"] for b in range(B)]
    _dbg("attn0T", attn0T)

    xn0T = [_t(xn0[b]) for b in range(B)]
    nc = build_shift()
    maps = [dict(x=xn0T[b], mur=_col(inp["L0_ffn_mu_r"]), muk=_col(inp["L0_ffn_mu_k"]))
            for b in range(B)]
    r = _run(nc, maps)
    xrT = [r[b]["xr"] for b in range(B)]
    xkT = [r[b]["xk"] for b in range(B)]
    _dbg("xrT", xrT); _dbg("xkT", xkT)

    nc = build_mm(D, D, act=[(0, D, AF.Sigmoid)], outs=[("o", 0, D)], bias=False)
    maps = [dict(at=xrT[b], w=_t(np.asarray(inp["L0_ffn_W_r"], np.float32)))
            for b in range(B)]
    r = _run(nc, maps)
    rT = [r[b]["o"] for b in range(B)]
    _dbg("rT", rT)

    nc = build_mm(D, D, act=[(0, D, AF.Relu)], relu_sq=True,
                  outs=[("o", 0, D)], bias=False)
    maps = [dict(at=xkT[b], w=_t(np.asarray(inp["L0_ffn_W_k"], np.float32)))
            for b in range(B)]
    r = _run(nc, maps)
    kactT = [r[b]["o"] for b in range(B)]
    _dbg("kactT", kactT)

    xT = [_t(np.ascontiguousarray(x[b])) for b in range(B)]
    nc = build_mm(D, D, act=None, bias=False, NT=256,
                  extras=[("er", ALU.mult), ("ea", ALU.add), ("en", ALU.add), ("ex", ALU.add)],
                  outs=[("o", 0, D)])
    maps = [dict(at=kactT[b], w=_t(np.asarray(inp["L0_ffn_W_v"], np.float32)),
                 er=rT[b], ea=attn0T[b], en=xn0T[b], ex=xT[b]) for b in range(B)]
    r = _run(nc, maps)
    x1T = [r[b]["o"] for b in range(B)]
    _dbg("x1T", x1T)

    nc = build_ln(False)
    maps = [dict(x=_t(x1T[b]), gam=_bc(inp["L1_attn_norm_w"]),
                 bet=_bc(inp["L1_attn_norm_b"])) for b in range(B)]
    r = _run(nc, maps)
    xn1T = [_t(r[b]["xn"]) for b in range(B)]
    xlnT = [_t(r[b]["z"]) for b in range(B)]
    _dbg("xn1T", xn1T); _dbg("xlnT", xlnT)

    nc = build_mm(D, 2 * D, act=None, outs=[("o", 0, 2 * D)], bias=True)
    maps = [dict(at=xlnT[b], w=_t(np.asarray(inp["L1_attn_W_in"], np.float32)),
                 bias=_col(inp["L1_attn_b_in"])) for b in range(B)]
    r = _run(nc, maps)
    hT = [r[b]["o"] for b in range(B)]
    _dbg("hT", hT)

    wau = np.zeros((2 * D, 128), np.float32)
    wau[:, 0:16] = np.asarray(inp["L1_attn_A"], np.float32)
    wau[:, 32:48] = np.asarray(inp["L1_attn_B_ssm"], np.float32)
    nc = build_mm(2 * D, 128, act=[(0, 16, AF.Sigmoid), (32, 48, AF.Copy)],
                  outs=[("a", 0, 16), ("u2", 32, 48)], bias=False)
    maps = [dict(at=hT[b], w=wau) for b in range(B)]
    r = _run(nc, maps)
    a_sc = np.empty((128, L), np.float32)
    u_sc = np.empty((128, L), np.float32)
    for b in range(B):
        a_sc[b * 16:(b + 1) * 16] = r[b]["a"]
        u_sc[b * 16:(b + 1) * 16] = r[b]["u2"]

    nc = build_scan1()
    maps = [dict(a=a_sc, u=u_sc) for _ in range(NCORES)]
    r = _run(nc, maps)
    states = r[0]["states"]
    _dbg("a_sc", a_sc); _dbg("u_sc", u_sc); _dbg("states", states)

    nc = build_y_out()
    maps = [dict(states=np.ascontiguousarray(states[b * 16:(b + 1) * 16]),
                 c_ssm=np.asarray(inp["L1_attn_C_ssm"], np.float32),
                 woT=_t(np.asarray(inp["L1_attn_W_out"], np.float32)),
                 b_out=_col(inp["L1_attn_b_out"]),
                 x1=x1T[b], xn1=xn1T[b]) for b in range(B)]
    r = _run(nc, maps)
    s1T = [r[b]["s1"] for b in range(B)]
    _dbg("s1T", s1T)

    wfi = np.concatenate([_t(np.asarray(inp["L1_ffn_wf"], np.float32)),
                          _t(np.asarray(inp["L1_ffn_wi"], np.float32))], axis=1)
    wco = np.concatenate([_t(np.asarray(inp["L1_ffn_wc"], np.float32)),
                          _t(np.asarray(inp["L1_ffn_wo"], np.float32))], axis=1)
    nc = build_mm(D, 2 * D, act=None, outs=[("o", 0, 2 * D)], bias=False, out_np_dtype=BF)
    maps = [dict(at=xn1T[b], w=wfi) for b in range(B)]
    r = _run(nc, maps)
    xwfi = [r[b]["o"] for b in range(B)]
    _dbg("xwfi", xwfi)
    maps = [dict(at=xn1T[b], w=wco) for b in range(B)]
    r = _run(nc, maps)
    xwco = [r[b]["o"] for b in range(B)]
    _dbg("xwco", xwco)

    u3 = np.concatenate([_t(np.asarray(inp["L1_ffn_uf"], np.float32)),
                         _t(np.asarray(inp["L1_ffn_ui"], np.float32)),
                         _t(np.asarray(inp["L1_ffn_uc"], np.float32))], axis=1)
    b3 = _col(np.concatenate([np.asarray(inp["L1_ffn_bf"], np.float32),
                              np.asarray(inp["L1_ffn_bi"], np.float32),
                              np.asarray(inp["L1_ffn_bc"], np.float32)]))
    c0 = np.ascontiguousarray(
        np.broadcast_to(np.asarray(inp["L1_ffn_in_c"], np.float32)[:, None], (D, 1024)))
    ident = np.eye(128, dtype=np.float32)
    nc = build_ffn1_loop()
    maps = []
    for k in range(NCORES):
        xw3 = np.empty((3 * D, B * 1024), BF)
        for t in range(B):
            xw3[0:D, t * 1024:(t + 1) * 1024] = xwfi[t][0:D, k * 1024:(k + 1) * 1024]
            xw3[D:2 * D, t * 1024:(t + 1) * 1024] = xwfi[t][D:2 * D, k * 1024:(k + 1) * 1024]
            xw3[2 * D:3 * D, t * 1024:(t + 1) * 1024] = xwco[t][0:D, k * 1024:(k + 1) * 1024]
        maps.append(dict(xw=xw3, u3=u3, b3=b3, c0=c0, ident=ident))
    r = _run(nc, maps)
    cf = [r[k]["cf"] for k in range(NCORES)]
    _dbg("cf", cf)

    nc = build_ffn1_out()
    maps = []
    for k in range(NCORES):
        xwo = np.empty((D, B * 1024), BF)
        s1k = np.empty((D, B * 1024), np.float32)
        for t in range(B):
            xwo[:, t * 1024:(t + 1) * 1024] = xwco[t][D:2 * D, k * 1024:(k + 1) * 1024]
            s1k[:, t * 1024:(t + 1) * 1024] = s1T[t][:, k * 1024:(k + 1) * 1024]
        maps.append(dict(cf=cf[k], uoT=_t(np.asarray(inp["L1_ffn_uo"], np.float32)),
                         xwo=xwo, bo=_col(inp["L1_ffn_bo"]), s1=s1k))
    r = _run(nc, maps)

    out = np.empty((B, L, D), np.float32)
    for k in range(NCORES):
        x2 = r[k]["x2"]
        for t in range(B):
            out[t, k * 1024:(k + 1) * 1024, :] = x2[:, t * 1024:(t + 1) * 1024].T
    return out


def _scan0_pack(chT, c):
    sl = np.stack([chT[b][c * 128:(c + 1) * 128] for b in range(B)])
    pad = np.zeros((B, 128, SC_W), BF)
    up = np.concatenate([pad, sl], axis=2)
    w = np.lib.stride_tricks.sliding_window_view(up, SC_STEPS, axis=2)[:, :, ::SC_LC, :]
    assert w.shape == (B, 128, SC_K, SC_STEPS)
    out = np.ascontiguousarray(w.transpose(1, 3, 0, 2)).reshape(128, -1)
    return out


def _scan0_unpack(hs_list):
    res = []
    for b in range(B):
        arr = np.empty((D, L), BF)
        for c in range(NCORES):
            h = hs_list[c].reshape(SC_LC, B, SC_K, 128)
            h = hs_list[c].reshape(128, SC_LC, B, SC_K)
            blk = h[:, :, b, :]
            arr[c * 128:(c + 1) * 128] = blk.transpose(0, 2, 1).reshape(128, L)
        res.append(arr)
    return res
